# version 14
# baseline (speedup 1.0000x reference)
import contextlib
import numpy as np
import ml_dtypes
import concourse.bass as bass
import concourse.mybir as mybir
from concourse.bass_utils import run_bass_kernel_spmd

F32 = mybir.dt.float32
BF16 = mybir.dt.bfloat16
I32 = mybir.dt.int32
AF = mybir.ActivationFunctionType
ALU = mybir.AluOpType
AX = mybir.AxisListType
BF = ml_dtypes.bfloat16

NCORES = 8
T = 1024
D = 4096
NT = 8
NKC = 32
NSW = 8
NCH = 32
RING = 8
PF = 6
EPS = 1e-5
NEG_BIG = -1.0e30


class Ctr:
    def __init__(self, sem, step):
        self.sem, self.step, self.v = sem, step, 0

    def bump(self):
        self.v += self.step
        return (self.sem, self.v)


class Stream:
    def __init__(self, name):
        self.name = name
        self.ops = []
        self.hwm = {}

    def op(self, fn, waits=(), inc=None):
        ws = []
        for tok in waits:
            if tok is None:
                continue
            sem, val = tok
            k = id(sem)
            if self.hwm.get(k, -1) < val:
                self.hwm[k] = val
                ws.append((sem, val))
        tok = None
        incs = []
        if inc is not None:
            tok = inc.bump()
            incs = [(inc.sem, inc.step)]
        self.ops.append((ws, fn, incs))
        return tok

    def emit(self, eng):
        for ws, fn, incs in self.ops:
            for sem, val in ws:
                eng.wait_ge(sem, val)
            inst = fn(eng) if fn is not None else None
            for sem, n in incs:
                inst.then_inc(sem, n)


def build_program():
    nc = bass.Bass("TRN2", target_bir_lowering=False,
                   detect_race_conditions=False)

    dx = nc.dram_tensor("x", [T, D], F32, kind="ExternalInput").ap()
    dw = nc.dram_tensor("w_attn", [D, D], F32, kind="ExternalInput").ap()
    dmask = nc.dram_tensor("mask", [D, D], I32, kind="ExternalInput").ap()
    dwb = nc.dram_tensor("w_base", [D, D], F32, kind="ExternalInput").ap()
    dbb = nc.dram_tensor("b_base", [1, D], F32, kind="ExternalInput").ap()
    dwr = nc.dram_tensor("w_router", [8, D], F32, kind="ExternalInput").ap()
    da = nc.dram_tensor("a_all", [128, D], F32, kind="ExternalInput").ap()
    dball = nc.dram_tensor("b_all", [128, D], F32, kind="ExternalInput").ap()
    dcI = nc.dram_tensor("c_ident", [128, 128], BF16, kind="ExternalInput").ap()
    dcon = nc.dram_tensor("c_ones", [1, 128], BF16, kind="ExternalInput").ap()
    dcsx = nc.dram_tensor("c_sexp", [8, 128], BF16, kind="ExternalInput").ap()
    dout = nc.dram_tensor("y", [T, D], F32, kind="ExternalOutput").ap()

    wmd = nc.dram_tensor("wmd", [D, D], BF16, kind="Internal").ap()
    wbd = nc.dram_tensor("wbd", [D, D], BF16, kind="Internal").ap()
    boutd = nc.dram_tensor("boutd", [2, 128, 128], BF16, kind="Internal").ap()
    bttd = nc.dram_tensor("bttd", [2, 32, 128], BF16, kind="Internal").ap()

    es = contextlib.ExitStack()
    with es:
        def sb(n, sh, dt, **kw):
            return es.enter_context(nc.sbuf_tensor(n, sh, dt, **kw))
        xtok = sb("xtok", [128, NT * D], BF16)
        xT = sb("xT", [128, NKC * T], BF16, is_transpose_dest=True)
        wst = sb("wst", [128, 2 * 2048], BF16)
        wring = sb("wring", [128, RING * 512], BF16, is_transpose_dest=True)
        mst = sb("mst", [128, 2 * 1024], BF16)
        bout = sb("bout", [128, 2 * 128], BF16)
        boutT = sb("boutT", [128, 2 * 128], BF16, is_transpose_dest=True)
        bsum = sb("bsum", [128, 2 * 4], F32)
        bts = sb("bts", [128, 2 * 32], BF16)
        btt = sb("btt", [128, 2 * 32], BF16)
        expp = sb("expp", [128, 2 * 128], BF16)
        ball = sb("ball", [128, D], BF16)
        aallT = sb("aallT", [128, D], BF16)
        hp = sb("hp", [128, T], BF16)
        wrT = sb("wrT", [128, NKC * 8], BF16)
        bbias = sb("bbias", [1, D], BF16)
        ost = sb("ost", [128, 2 * 512], F32)
        cI = sb("cI", [128, 128], BF16)
        cones = sb("cones", [1, 128], BF16)
        csexp = sb("csexp", [8, 128], BF16)
        lg = sb("lg", [128, 64], F32)
        c1b = sb("c1b", [128, 64], F32)
        c2b = sb("c2b", [128, 64], F32)
        mkd = sb("mkd", [128, 64], F32)
        sg1 = sb("sg1", [128, 64], F32)
        sg2 = sb("sg2", [128, 64], F32)
        t1b = sb("t1b", [128, 64], F32)
        wq = sb("wq", [128, 64], F32)
        wqb = sb("wqb", [128, 64], BF16)
        wT8 = sb("wT8", [8, T], BF16)
        wexp = sb("wexp", [128, T], BF16)
        m1r = sb("m1r", [128, 8], F32)
        m2r = sb("m2r", [128, 8], F32)
        n1r = sb("n1r", [128, 8], F32)
        is1 = sb("is1", [128, 8], F32)
        ng1 = sb("ng1", [128, 8], F32)
        ng2 = sb("ng2", [128, 8], F32)
        sumr = sb("sumr", [128, 8], F32)
        sq2 = sb("sq2", [128, 16], F32)
        sumsq = sb("sumsq", [128, 8], F32)
        mmean = sb("mmean", [128, 8], F32)
        msq = sb("msq", [128, 8], F32)
        varr = sb("varr", [128, 8], F32)
        stdd = sb("stdd", [128, 8], F32)
        rstd = sb("rstd", [128, 8], F32)
        nmr = sb("nmr", [128, 8], F32)

        PALL = es.enter_context(nc.psum_tensor("PALL", [128, D], F32))

        qg = Ctr(es.enter_context(nc.semaphore("qg")), 16)
        qs = Ctr(es.enter_context(nc.semaphore("qs")), 16)
        qa = Ctr(es.enter_context(nc.semaphore("qa")), 16)
        cg = Ctr(es.enter_context(nc.semaphore("cg")), 1)
        cv = Ctr(es.enter_context(nc.semaphore("cv")), 1)
        ca = Ctr(es.enter_context(nc.semaphore("ca")), 1)
        cp = Ctr(es.enter_context(nc.semaphore("cp")), 1)

        G, S, A, V, P = (Stream(x) for x in "GSAVP")

        def bankview(b, lo=0, hi=512):
            return PALL[:, b * 512 + lo: b * 512 + hi]

        bankfree = {b: None for b in range(8)}
        tk = {}

        for t in range(NT):
            tk[f"xc{t}"] = G.op(lambda g, t=t: g.dma_start(
                out=xtok[:, t * D:(t + 1) * D], in_=dx[t * 128:(t + 1) * 128, :]),
                inc=qg)
        tk["bb"] = G.op(lambda g: g.dma_start(out=bbias[:], in_=dbb), inc=qg)

        tk["ball"] = G.op(lambda g: g.dma_start(out=ball[:], in_=dball), inc=qg)
        for h in range(2):
            tk[f"aL{h}"] = G.op(lambda g, h=h: g.dma_start(
                out=wring[:, h * 2048:(h + 1) * 2048],
                in_=da[:, h * 2048:(h + 1) * 2048]),
                inc=qg)

        tk["cI"] = S.op(lambda s: s.dma_start(out=cI[:], in_=dcI), inc=qs)
        tk["cones"] = S.op(lambda s: s.dma_start(out=cones[:], in_=dcon), inc=qs)
        tk["csexp"] = S.op(lambda s: s.dma_start(out=csexp[:], in_=dcsx), inc=qs)

        for g8 in range(8):
            h = g8 // 4
            last = None
            for j in range(4):
                kc = g8 * 4 + j
                off = (kc % 16) * 128 + h * 2048
                last = P.op(lambda t_, off=off, j=j: t_.matmul(
                    bankview(4, j * 128, (j + 1) * 128),
                    wring[:, off:off + 128], cI[:],
                    start=(j == 0), stop=(j == 3)),
                    waits=[tk["aL0"], tk["aL1"], tk["cI"], bankfree[4]],
                    inc=(cp if j == 3 else None))
            dst = aallT[:].rearrange("p (a b) -> p a b", b=128)[:, g8 * 4:(g8 + 1) * 4, :]
            src = bankview(4).rearrange("p (a b) -> p a b", b=128)
            bankfree[4] = V.op(lambda v, dst=dst, src=src: v.tensor_copy(dst, src),
                               waits=[last], inc=cv)
        tk["aallT"] = bankfree[4]

        for h in range(2):
            tk[f"wrL{h}"] = G.op(lambda g, h=h: g.dma_start(
                out=wring[0:8, h * 2048:h * 2048 + 2048],
                in_=dwr[:, h * 2048:(h + 1) * 2048]),
                waits=[tk["aallT"]], inc=qg)
        last = None
        for kc in range(NKC):
            h = kc // 16
            off = (kc % 16) * 128 + h * 2048
            last = P.op(lambda t_, kc=kc, off=off: t_.matmul(
                bankview(4, kc * 8, kc * 8 + 8),
                wring[0:8, off:off + 128], cI[0:8, 0:8],
                start=(kc == 0), stop=(kc == 31)),
                waits=[tk["wrL0"], tk["wrL1"], bankfree[4]],
                inc=(cp if kc == 31 else None))
        tk["wrT"] = bankfree[4] = V.op(
            lambda v: v.tensor_copy(wrT[:], bankview(4, 0, 256)),
            waits=[last], inc=cv)

        tk["btsz"] = V.op(lambda v: v.memset(bts[:], 0.0), inc=cv)



        mst_free = [None, None]
        wst_free = [None, None]
        bout_free = [None, None]
        boutT_free = [None, None]
        btt_free = [None, None]
        expp_free = [None, None]
        chunk_wmw = {}

        def plan_chunk(c):
            sl = c % 2
            tree_eng, tree_ctr = (G, cg) if c % 2 == 0 else (V, cv)
            btoks = []
            mLp = [None, None]
            for pr in range(2):
                mLp[pr] = G.op(lambda g, c=c, pr=pr: g.dma_start(
                    out=mst[:],
                    in_=dmask[c * 128:(c + 1) * 128, pr * 2048:(pr + 1) * 2048]),
                    waits=[mst_free[0], mst_free[1]], inc=qg)
                tk.setdefault(f"mfirst{c}", mLp[pr])
                for q in (2 * pr, 2 * pr + 1):
                    qsl = q % 2
                    mL = mLp[pr]
                    v3 = mst[:, qsl * 1024:(qsl + 1) * 1024].rearrange(
                        "p (a b) -> p a b", b=32)
                    if tree_eng is V:
                        def _red(e, v3=v3, c=c, q=q, sl=sl):
                            with e.bass.allow_low_precision(
                                    reason="window sums of ints, exact bf16"):
                                return e.reduce_sum(
                                    bout[:, sl * 128 + q * 32:
                                         sl * 128 + q * 32 + 32],
                                    v3, axis=AX.X)
                        tok = tree_eng.op(_red, waits=[mL, bout_free[sl]],
                                          inc=tree_ctr)
                    else:
                        tok = None
                        for step in (16, 8, 4, 2, 1):
                            w = [mL] if tok is None else [mL, tok]
                            tok = tree_eng.op(lambda e, v3=v3, step=step:
                                              e.tensor_tensor(
                                                  v3[:, :, 0:step],
                                                  v3[:, :, 0:step],
                                                  v3[:, :, step:2 * step],
                                                  ALU.add),
                                              waits=w, inc=tree_ctr)
                        tok = tree_eng.op(lambda e, v3=v3, c=c, q=q, sl=sl:
                                          e.tensor_copy(
                                              bout[:, sl * 128 + q * 32:
                                                   sl * 128 + q * 32 + 32],
                                              v3[:, :, 0]),
                                          waits=[tok, bout_free[sl]],
                                          inc=tree_ctr)
                    mst_free[qsl] = tok
                    btoks.append(tok)
            if tree_eng is V:
                btoks.append(V.op(lambda v: v.memset(bsum[:, sl * 4:sl * 4 + 1],
                                                     0.0),
                                  waits=[btoks[-1]], inc=cv))
            t_bw = A.op(lambda a, c=c: a.dma_start(
                out=boutd[sl], in_=bout[:, sl * 128:sl * 128 + 128]),
                waits=btoks, inc=qa)
            t_btl = S.op(lambda s, c=c: s.dma_start(
                out=boutT[:, sl * 128:sl * 128 + 128], in_=boutd[sl],
                transpose=True), waits=[t_bw, boutT_free[sl]], inc=qs)
            bv = boutT[:, sl * 128:sl * 128 + 128].rearrange("p (a b) -> p a b", b=32)
            t_rs = V.op(lambda v, bv=bv, c=c: v.reduce_sum(
                bsum[:, sl * 4:sl * 4 + 4], bv, axis=AX.X),
                waits=[t_btl], inc=cv)
            boutT_free[sl] = t_rs
            t_th = V.op(lambda v, c=c: v.tensor_scalar(
                bts[:, sl * 32:sl * 32 + 4],
                bsum[:, sl * 4:sl * 4 + 4], 0.0, None, ALU.is_gt),
                waits=[t_rs, tk["btsz"]], inc=cv)
            t_tt = V.op(lambda v, c=c: v.transpose(
                btt[:, sl * 32:sl * 32 + 32], bts[:, sl * 32:sl * 32 + 32]),
                waits=[t_th, btt_free[sl]], inc=cv)
            t_tw = A.op(lambda a, c=c: a.dma_start(
                out=bttd[sl].rearrange("nb (qq j) -> qq nb j", j=32),
                in_=btt[:, sl * 32:sl * 32 + 32]),
                waits=[t_tt], inc=qa)
            btt_free[sl] = t_tw
            src = bttd[sl][0:4, :].unsqueeze(1).broadcast_to([4, 32, 128])
            t_ex = A.op(lambda a, c=c, src=src: a.dma_start(
                out=expp[:, sl * 128:sl * 128 + 128], in_=src),
                waits=[t_tw, expp_free[sl]], inc=qa)
            bout_free[sl] = t_bw
            mm_last = None
            for h in range(2):
                t_wl = G.op(lambda g, c=c, h=h: g.dma_start(
                    out=wst[:, h * 2048:(h + 1) * 2048],
                    in_=dw[c * 128:(c + 1) * 128, h * 2048:(h + 1) * 2048]),
                    waits=[wst_free[h]], inc=qg)
                wv = wst[:, h * 2048:(h + 1) * 2048].rearrange("p (a b) -> p a b", b=32)
                ev = expp[:, sl * 128 + h * 64:sl * 128 + h * 64 + 64] \
                    .unsqueeze(2).broadcast_to([128, 64, 32])
                mm_last = V.op(lambda v, wv=wv, ev=ev: v.tensor_tensor(
                    wv, wv, ev, ALU.mult), waits=[t_wl, t_ex], inc=cv)
                t_ww = A.op(lambda a, c=c, h=h: a.dma_start(
                    out=wmd[c * 128:(c + 1) * 128, h * 2048:(h + 1) * 2048],
                    in_=wst[:, h * 2048:(h + 1) * 2048]),
                    waits=[mm_last], inc=qa)
                wst_free[h] = t_ww
                chunk_wmw[(c, h)] = t_ww
            expp_free[sl] = mm_last

        for c in range(8):
            plan_chunk(c)

        for g8 in range(8):
            for tau in range(8):
                bk = 4 + (tau % 2)
                last = None
                for j in range(4):
                    kc = g8 * 4 + j
                    guard = tk[f"xc{tau + 1}"] if tau < 7 else tk["mfirst0"]
                    last = P.op(lambda t_, tau=tau, kc=kc, j=j, bk=bk: t_.matmul(
                        bankview(bk, j * 128, (j + 1) * 128),
                        xtok[:, tau * D + kc * 128:tau * D + kc * 128 + 128],
                        cI[:], start=(j == 0), stop=(j == 3)),
                        waits=[tk[f"xc{tau}"], guard, tk["cI"], bankfree[bk]],
                        inc=(cp if j == 3 else None))
                dst = xT[:].rearrange("p (a b) -> p a b", b=T)[
                    :, g8 * 4:(g8 + 1) * 4, tau * 128:(tau + 1) * 128]
                psrc = bankview(bk).rearrange("p (a b) -> p a b", b=128)
                tok = V.op(lambda v, dst=dst, psrc=psrc: v.tensor_copy(dst, psrc),
                           waits=[last], inc=cv)
                bankfree[bk] = tok
            tk[f"xTg{g8}"] = tok


        ringfree = [None] * RING
        ring_tok = {}

        def ring_load(dram, gidx, s, kc, extra):
            slot = gidx % RING
            w = list(extra) + [ringfree[slot]]
            if gidx < RING:
                w.append(tk["wrT"])
            ring_tok[gidx] = S.op(
                lambda sy, dram=dram, s=s, kc=kc, slot=slot: sy.dma_start(
                    out=wring[:, slot * 512:(slot + 1) * 512],
                    in_=dram[s * 512:(s + 1) * 512, kc * 128:(kc + 1) * 128],
                    transpose=True),
                waits=w, inc=qs)

        aev = {}
        next_chunk = 8
        for s in range(NSW):
            for i in range(NKC + PF):
                if i < NKC:
                    if s == 0:
                        ws = [chunk_wmw[(c, h)] for c in range(4) for h in range(2)]
                        ws = [chunk_wmw[(3, 1)]]
                    else:
                        ws = [chunk_wmw[(4 * s + 3, 1)]]
                    ring_load(wmd, s * NKC + i, s, i, ws)
                j = i - PF
                if 0 <= j < NKC:
                    gidx = s * NKC + j
                    slot = gidx % RING
                    tok = None
                    for tau in range(8):
                        w = [ring_tok[gidx]]
                        if s == 0:
                            w.append(tk[f"xTg{j // 4}"])
                        if j == 0:
                            w.append(bankfree[tau])
                        tok = P.op(lambda t_, tau=tau, j=j, slot=slot: t_.matmul(
                            bankview(tau),
                            xT[:, j * T + tau * 128:j * T + tau * 128 + 128],
                            wring[:, slot * 512:(slot + 1) * 512],
                            start=(j == 0), stop=(j == NKC - 1)),
                            waits=w, inc=(cp if tau == 7 else None))
                    ringfree[slot] = tok
                    if j == NKC - 1:
                        tk[f"amm{s}"] = tok
                if i in (2, 10, 18, 26) and next_chunk < NCH:
                    plan_chunk(next_chunk)
                    next_chunk += 1
                if 1 <= s <= 4 and i == 7:
                    for ii in (2 * (s - 1), 2 * (s - 1) + 1):
                        tk[f"wbd{ii}"] = G.op(lambda g, ii=ii: g.dma_start(
                            out=wbd[ii * 512:(ii + 1) * 512, :],
                            in_=dwb[ii * 512:(ii + 1) * 512, :]),
                            inc=qg)

            for tau in range(8):
                dst = xtok[:, tau * D + s * 512:tau * D + s * 512 + 512]
                tok = V.op(lambda v, tau=tau, dst=dst: v.scalar_tensor_tensor(
                    dst, bankview(tau), 1.0, dst, ALU.mult, ALU.add),
                    waits=[tk[f"amm{s}"]], inc=cv)
                aev[(s, tau)] = tok
                bankfree[tau] = tok

        ln1 = {}
        sqt = {}
        for tau in range(8):
            w_data = [aev[(NSW - 1, tau)]]
            for h in range(2):
                A.op(lambda a, tau=tau, h=h: a.activation(
                    PALL[:, 0:2048],
                    xtok[:, tau * D + h * 2048:tau * D + h * 2048 + 2048],
                    AF.Square, accum_out=sq2[:, 2 * tau + h:2 * tau + h + 1]),
                    waits=w_data + [bankfree[b] for b in range(4)], inc=ca)
            sqt[tau] = (ca.sem, ca.v)
        srt = {}
        for tau in range(8):
            srt[tau] = V.op(lambda v, tau=tau: v.reduce_sum(
                sumr[:, tau:tau + 1], xtok[:, tau * D:(tau + 1) * D], axis=AX.X),
                waits=[aev[(NSW - 1, tau)]], inc=cv)
        for tau in range(8):
            V.op(lambda v, tau=tau: v.tensor_tensor(
                sumsq[:, tau:tau + 1], sq2[:, 2 * tau:2 * tau + 1],
                sq2[:, 2 * tau + 1:2 * tau + 2], ALU.add),
                waits=[sqt[tau]], inc=cv)
        for tau in range(8):
            V.op(lambda v, tau=tau: v.tensor_scalar(
                mmean[:, tau:tau + 1], sumr[:, tau:tau + 1], 1.0 / D, None,
                ALU.mult), waits=[srt[tau]], inc=cv)
        for tau in range(8):
            V.op(lambda v, tau=tau: v.tensor_scalar(
                msq[:, tau:tau + 1], mmean[:, tau:tau + 1], mmean[:, tau:tau + 1],
                EPS, ALU.mult, ALU.subtract), inc=cv)
        vvt = {}
        for tau in range(8):
            vvt[tau] = V.op(lambda v, tau=tau: v.tensor_scalar(
                varr[:, tau:tau + 1], sumsq[:, tau:tau + 1], 1.0 / D,
                msq[:, tau:tau + 1], ALU.mult, ALU.subtract), inc=cv)
        sdt = {}
        for tau in range(8):
            sdt[tau] = A.op(lambda a, tau=tau: a.activation(
                stdd[:, tau:tau + 1], varr[:, tau:tau + 1], AF.Sqrt),
                waits=[vvt[tau]], inc=ca)
        for tau in range(8):
            V.op(lambda v, tau=tau: v.reciprocal(
                rstd[:, tau:tau + 1], stdd[:, tau:tau + 1]),
                waits=[sdt[tau]], inc=cv)
        for tau in range(8):
            V.op(lambda v, tau=tau: v.tensor_scalar(
                nmr[:, tau:tau + 1], mmean[:, tau:tau + 1],
                rstd[:, tau:tau + 1], -1.0, ALU.mult, ALU.mult), inc=cv)
        for tau in range(8):
            ln1[tau] = V.op(lambda v, tau=tau: v.tensor_scalar(
                xtok[:, tau * D:(tau + 1) * D], xtok[:, tau * D:(tau + 1) * D],
                rstd[:, tau:tau + 1], nmr[:, tau:tau + 1], ALU.mult, ALU.add),
                inc=cv)
        for b in range(4):
            bankfree[b] = (ca.sem, ca.v)

        y1T = {}
        for tau in range(8):
            for g8 in range(8):
                bk = 4 + (g8 % 2)
                last = None
                for j in range(4):
                    kc = g8 * 4 + j
                    last = P.op(lambda t_, tau=tau, kc=kc, j=j, bk=bk: t_.matmul(
                        bankview(bk, j * 128, (j + 1) * 128),
                        xtok[:, tau * D + kc * 128:tau * D + kc * 128 + 128],
                        cI[:], start=(j == 0), stop=(j == 3)),
                        waits=[ln1[tau], bankfree[bk], tk[f"amm{NSW - 1}"]],
                        inc=(cp if j == 3 else None))
                dst = xT[:].rearrange("p (a b) -> p a b", b=T)[
                    :, g8 * 4:(g8 + 1) * 4, tau * 128:(tau + 1) * 128]
                src = bankview(bk).rearrange("p (a b) -> p a b", b=128)
                tok = V.op(lambda v, dst=dst, src=src: v.tensor_copy(dst, src),
                           waits=[last], inc=cv)
                y1T[(tau, g8)] = tok
                bankfree[bk] = tok

        tok = None
        for tau in range(8):
            for kc in range(NKC):
                tok = P.op(lambda t_, tau=tau, kc=kc: t_.matmul(
                    PALL[:, 3072 + tau * 8:3072 + tau * 8 + 8],
                    xT[:, kc * T + tau * 128:kc * T + tau * 128 + 128],
                    wrT[:, kc * 8:(kc + 1) * 8],
                    start=(kc == 0), stop=(kc == NKC - 1)),
                    waits=[y1T[(tau, kc // 4)], tk["wrT"],
                           bankfree[6] if (tau == 0 and kc == 0) else None],
                    inc=(cp if (tau == 7 and kc == NKC - 1) else None))
        t_lg = V.op(lambda v: v.tensor_copy(lg[:], PALL[:, 3072:3136]),
                    waits=[tok], inc=cv)
        bankfree[6] = t_lg
        wtok = {}
        for tau in range(8):
            c8 = slice(tau * 8, tau * 8 + 8)
            tc = slice(tau, tau + 1)
            t1 = V.op(lambda v, c8=c8, tc=tc: v.reduce_max(
                m1r[:, tc], lg[:, c8], axis=AX.X), waits=[t_lg], inc=cv)
            t2 = V.op(lambda v, c8=c8, tc=tc: v.tensor_scalar(
                c1b[:, c8], lg[:, c8], m1r[:, tc], None, ALU.is_equal),
                waits=[t1], inc=cv)
            t3 = V.op(lambda v, c8=c8: v.scalar_tensor_tensor(
                mkd[:, c8], c1b[:, c8], NEG_BIG, lg[:, c8], ALU.mult, ALU.add),
                waits=[t2], inc=cv)
            t4 = V.op(lambda v, c8=c8, tc=tc: v.reduce_max(
                m2r[:, tc], mkd[:, c8], axis=AX.X), waits=[t3], inc=cv)
            t5 = V.op(lambda v, c8=c8, tc=tc: v.reduce_sum(
                n1r[:, tc], c1b[:, c8], axis=AX.X), waits=[t2], inc=cv)
            t6 = V.op(lambda v, tc=tc: v.tensor_scalar(
                is1[:, tc], n1r[:, tc], 1.0, None, ALU.is_equal),
                waits=[t5], inc=cv)
            t7 = V.op(lambda v, tc=tc: v.tensor_scalar(
                ng2[:, tc], m2r[:, tc], -1.0, None, ALU.mult), waits=[t4], inc=cv)
            t8 = V.op(lambda v, tc=tc: v.tensor_scalar(
                ng1[:, tc], m1r[:, tc], -1.0, None, ALU.mult), waits=[t1], inc=cv)
            t9 = A.op(lambda a, c8=c8, tc=tc: a.activation(
                sg1[:, c8], lg[:, c8], AF.Sigmoid, bias=ng2[:, tc]),
                waits=[t7], inc=ca)
            t10 = A.op(lambda a, c8=c8, tc=tc: a.activation(
                sg2[:, c8], lg[:, c8], AF.Sigmoid, bias=ng1[:, tc]),
                waits=[t8], inc=ca)
            t11 = V.op(lambda v, c8=c8, tc=tc: v.tensor_scalar(
                c2b[:, c8], mkd[:, c8], m2r[:, tc], None, ALU.is_equal),
                waits=[t4], inc=cv)
            t12 = V.op(lambda v, c8=c8, tc=tc: v.tensor_scalar(
                t1b[:, c8], sg1[:, c8], -0.5, is1[:, tc], ALU.add, ALU.mult),
                waits=[t9, t6], inc=cv)
            t13 = V.op(lambda v, c8=c8: v.tensor_scalar(
                t1b[:, c8], t1b[:, c8], 0.5, None, ALU.add), waits=[t12], inc=cv)
            t14 = V.op(lambda v, c8=c8: v.tensor_tensor(
                t1b[:, c8], c1b[:, c8], t1b[:, c8], ALU.mult), waits=[t13], inc=cv)
            t15 = V.op(lambda v, c8=c8, tc=tc: v.scalar_tensor_tensor(
                c2b[:, c8], c2b[:, c8], is1[:, tc], sg2[:, c8], ALU.mult, ALU.mult),
                waits=[t11, t10], inc=cv)
            t16 = V.op(lambda v, c8=c8: v.tensor_tensor(
                wq[:, c8], t1b[:, c8], c2b[:, c8], ALU.add),
                waits=[t14, t15], inc=cv)
            wtok[tau] = V.op(lambda v, c8=c8: v.tensor_copy(wqb[:, c8], wq[:, c8]),
                             waits=[t16], inc=cv)

        tok = None
        for tau in range(8):
            bk_off = 3072 + tau * 128
            tok = P.op(lambda t_, tau=tau, bk_off=bk_off: t_.matmul(
                PALL[0:8, bk_off:bk_off + 128],
                wqb[:, tau * 8:(tau + 1) * 8], cI[:],
                start=(tau in (0, 4)), stop=(tau in (3, 7))),
                waits=[wtok[tau], bankfree[6], bankfree[7]],
                inc=(cp if tau == 7 else None))
        t_wT = V.op(lambda v: v.tensor_copy(wT8[:], PALL[0:8, 3072:4096]),
                    waits=[tok], inc=cv)
        bankfree[6] = bankfree[7] = t_wT
        xtoks = []
        for j in range(2):
            tok = P.op(lambda t_, j=j: t_.matmul(
                PALL[:, 3072 + j * 512:3072 + (j + 1) * 512],
                csexp[:], wT8[:, j * 512:(j + 1) * 512],
                start=True, stop=True),
                waits=[t_wT, tk["csexp"], bankfree[6 + j]], inc=cp)
            xtoks.append(tok)
        t_we = V.op(lambda v: v.tensor_copy(wexp[:], PALL[:, 3072:4096]),
                    waits=xtoks, inc=cv)
        bankfree[6] = bankfree[7] = t_we
        tok = None
        for kc in range(NKC):
            for j in range(2):
                tok = P.op(lambda t_, kc=kc, j=j: t_.matmul(
                    PALL[:, 3072 + j * 512:3072 + (j + 1) * 512],
                    aallT[:, kc * 128:(kc + 1) * 128],
                    xT[:, kc * T + j * 512:kc * T + j * 512 + 512],
                    start=(kc == 0), stop=(kc == NKC - 1)),
                    waits=[tk["aallT"], y1T[(7, 7)], t_we],
                    inc=(cp if (kc == NKC - 1 and j == 1) else None))
        hp_last = None
        for j in range(2):
            hp_last = V.op(lambda v, j=j: v.tensor_tensor(
                hp[:, j * 512:(j + 1) * 512],
                PALL[:, 3072 + j * 512:3072 + (j + 1) * 512],
                wexp[:, j * 512:(j + 1) * 512], ALU.mult),
                waits=[tok], inc=cv)
        bankfree[6] = bankfree[7] = hp_last
        tk["hp"] = hp_last

        fev = {}
        for s in range(NSW):
            for i in range(NKC):
                gidx = 256 + s * NKC + i
                ring_load(wbd, gidx, s, i, [tk[f"wbd{s}"]])
                if i == 0:
                    for tau in range(8):
                        P.op(lambda t_, tau=tau, s=s: t_.matmul(
                            bankview(tau), cones[:],
                            bbias[0:1, s * 512:(s + 1) * 512],
                            start=True, stop=False),
                            waits=[tk["bb"], tk["cones"], bankfree[tau]], inc=None)
                j = i - PF
                if 0 <= j < NKC:
                    gj = 256 + s * NKC + j
                    slot = gj % RING
                    tok = None
                    for tau in range(8):
                        tok = P.op(lambda t_, tau=tau, j=j, slot=slot: t_.matmul(
                            bankview(tau),
                            xT[:, j * T + tau * 128:j * T + tau * 128 + 128],
                            wring[:, slot * 512:(slot + 1) * 512],
                            start=False, stop=False),
                            waits=[ring_tok[gj]], inc=(cp if tau == 7 else None))
                    ringfree[slot] = tok
            for j in range(NKC - PF, NKC):
                gj = 256 + s * NKC + j
                slot = gj % RING
                tok = None
                for tau in range(8):
                    tok = P.op(lambda t_, tau=tau, j=j, slot=slot: t_.matmul(
                        bankview(tau),
                        xT[:, j * T + tau * 128:j * T + tau * 128 + 128],
                        wring[:, slot * 512:(slot + 1) * 512],
                        start=False, stop=False),
                        waits=[ring_tok[gj]], inc=(cp if tau == 7 else None))
                ringfree[slot] = tok
            for tau in range(8):
                tok = P.op(lambda t_, tau=tau, s=s: t_.matmul(
                    bankview(tau), hp[:, tau * 128:(tau + 1) * 128],
                    ball[:, s * 512:(s + 1) * 512],
                    start=False, stop=True),
                    waits=[tk["hp"], tk["ball"]], inc=(cp if tau == 7 else None))
            tk[f"fmm{s}"] = tok
            for tau in range(8):
                dst = xtok[:, tau * D + s * 512:tau * D + s * 512 + 512]
                tok = V.op(lambda v, tau=tau, dst=dst: v.scalar_tensor_tensor(
                    dst, bankview(tau), 1.0, dst, ALU.mult, ALU.add),
                    waits=[tk[f"fmm{s}"]], inc=cv)
                fev[(s, tau)] = tok
                bankfree[tau] = tok

        ost_free = [None, None]
        sqt = {}
        for tau in range(8):
            w_data = [fev[(NSW - 1, tau)], fev[(NSW - 1, 7)]]
            for h in range(2):
                A.op(lambda a, tau=tau, h=h: a.activation(
                    PALL[:, 0:2048],
                    xtok[:, tau * D + h * 2048:tau * D + h * 2048 + 2048],
                    AF.Square, accum_out=sq2[:, 2 * tau + h:2 * tau + h + 1]),
                    waits=w_data + [bankfree[b] for b in range(4)], inc=ca)
            sqt[tau] = (ca.sem, ca.v)
        srt = {}
        for tau in range(8):
            srt[tau] = V.op(lambda v, tau=tau: v.reduce_sum(
                sumr[:, tau:tau + 1], xtok[:, tau * D:(tau + 1) * D], axis=AX.X),
                waits=[fev[(NSW - 1, tau)]], inc=cv)
        for tau in range(8):
            V.op(lambda v, tau=tau: v.tensor_tensor(
                sumsq[:, tau:tau + 1], sq2[:, 2 * tau:2 * tau + 1],
                sq2[:, 2 * tau + 1:2 * tau + 2], ALU.add),
                waits=[sqt[tau]], inc=cv)
        for tau in range(8):
            V.op(lambda v, tau=tau: v.tensor_scalar(
                mmean[:, tau:tau + 1], sumr[:, tau:tau + 1], 1.0 / D, None,
                ALU.mult), waits=[srt[tau]], inc=cv)
        for tau in range(8):
            V.op(lambda v, tau=tau: v.tensor_scalar(
                msq[:, tau:tau + 1], mmean[:, tau:tau + 1], mmean[:, tau:tau + 1],
                EPS, ALU.mult, ALU.subtract), inc=cv)
        vvt = {}
        for tau in range(8):
            vvt[tau] = V.op(lambda v, tau=tau: v.tensor_scalar(
                varr[:, tau:tau + 1], sumsq[:, tau:tau + 1], 1.0 / D,
                msq[:, tau:tau + 1], ALU.mult, ALU.subtract), inc=cv)
        sdt = {}
        for tau in range(8):
            sdt[tau] = A.op(lambda a, tau=tau: a.activation(
                stdd[:, tau:tau + 1], varr[:, tau:tau + 1], AF.Sqrt),
                waits=[vvt[tau]], inc=ca)
        for tau in range(8):
            V.op(lambda v, tau=tau: v.reciprocal(
                rstd[:, tau:tau + 1], stdd[:, tau:tau + 1]),
                waits=[sdt[tau]], inc=cv)
        nmt = {}
        for tau in range(8):
            nmt[tau] = V.op(lambda v, tau=tau: v.tensor_scalar(
                nmr[:, tau:tau + 1], mmean[:, tau:tau + 1],
                rstd[:, tau:tau + 1], -1.0, ALU.mult, ALU.mult), inc=cv)
        for tau in range(8):
            for q in range(8):
                slq = q % 2
                t_n = V.op(lambda v, tau=tau, q=q, slq=slq: v.tensor_scalar(
                    ost[:, slq * 512:(slq + 1) * 512],
                    xtok[:, tau * D + q * 512:tau * D + q * 512 + 512],
                    rstd[:, tau:tau + 1], nmr[:, tau:tau + 1],
                    ALU.mult, ALU.add),
                    waits=[nmt[tau], ost_free[slq]], inc=cv)
                t_st = A.op(lambda a, tau=tau, q=q, slq=slq: a.dma_start(
                    out=dout[tau * 128:(tau + 1) * 128, q * 512:q * 512 + 512],
                    in_=ost[:, slq * 512:(slq + 1) * 512]),
                    waits=[t_n], inc=qa)
                ost_free[slq] = t_st
        A.op(None, waits=[ost_free[0], ost_free[1]])

        blk = es.enter_context(nc.Block())
        blk.gpsimd(lambda g: G.emit(g))
        blk.sync(lambda s: S.emit(s))
        blk.scalar(lambda a: A.emit(a))
        blk.vector(lambda v: V.emit(v))
        blk.tensor(lambda t_: P.emit(t_))

    return nc


_NC_CACHE = []


def kernel(L0_attn_data, L0_attn_weight, L0_attn_mask, L0_ffn_W_base,
           L0_ffn_b_base, L0_ffn_W_router, L0_ffn_lora_A, L0_ffn_lora_B):
    x = np.ascontiguousarray(np.asarray(L0_attn_data, dtype=np.float32)
                             .reshape(8192, 4096))
    w = np.ascontiguousarray(np.asarray(L0_attn_weight, dtype=np.float32))
    mask = np.ascontiguousarray(np.asarray(L0_attn_mask, dtype=np.int32))
    wb = np.ascontiguousarray(np.asarray(L0_ffn_W_base, dtype=np.float32))
    bb = np.ascontiguousarray(np.asarray(L0_ffn_b_base, dtype=np.float32)
                              .reshape(1, 4096))
    wr = np.ascontiguousarray(np.asarray(L0_ffn_W_router, dtype=np.float32))
    a_all = np.ascontiguousarray(np.asarray(L0_ffn_lora_A, dtype=np.float32)
                                 .reshape(128, 4096))
    b_all = np.ascontiguousarray(np.asarray(L0_ffn_lora_B, dtype=np.float32)
                                 .transpose(0, 2, 1).reshape(128, 4096))
    c_ident = np.eye(128, dtype=np.float32).astype(BF)
    c_ones = np.ones((1, 128), dtype=np.float32).astype(BF)
    c_sexp = np.zeros((8, 128), dtype=np.float32)
    for r in range(128):
        c_sexp[r // 16, r] = 2.0
    c_sexp = c_sexp.astype(BF)

    if not _NC_CACHE:
        _NC_CACHE.append(build_program())
    nc = _NC_CACHE[0]

    in_maps = []
    for c in range(NCORES):
        in_maps.append({
            "x": np.ascontiguousarray(x[c * T:(c + 1) * T]),
            "w_attn": w, "mask": mask, "w_base": wb, "b_base": bb,
            "w_router": wr, "a_all": a_all, "b_all": b_all,
            "c_ident": c_ident, "c_ones": c_ones, "c_sexp": c_sexp,
        })
    res = run_bass_kernel_spmd(nc, in_maps, core_ids=list(range(NCORES)))
    out = np.concatenate([np.asarray(res.results[c]["y"]) for c in range(NCORES)],
                         axis=0)
    return out.reshape(1, 8192, 4096)


# version 15 (final)
# speedup vs baseline: 1.1005x; 1.1005x over previous
import contextlib
import numpy as np
import ml_dtypes
import concourse.bass as bass
import concourse.mybir as mybir
from concourse.bass_utils import run_bass_kernel_spmd

F32 = mybir.dt.float32
BF16 = mybir.dt.bfloat16
I32 = mybir.dt.int32
AF = mybir.ActivationFunctionType
ALU = mybir.AluOpType
AX = mybir.AxisListType
BF = ml_dtypes.bfloat16

NCORES = 8
T = 1024
D = 4096
NT = 8
NKC = 32
NSW = 8
NCH = 32
RING = 8
PF = 6
EPS = 1e-5
NEG_BIG = -1.0e30


class Ctr:
    def __init__(self, sem, step):
        self.sem, self.step, self.v = sem, step, 0

    def bump(self):
        self.v += self.step
        return (self.sem, self.v)


class Stream:
    def __init__(self, name):
        self.name = name
        self.ops = []
        self.hwm = {}

    def op(self, fn, waits=(), inc=None):
        ws = []
        for tok in waits:
            if tok is None:
                continue
            sem, val = tok
            k = id(sem)
            if self.hwm.get(k, -1) < val:
                self.hwm[k] = val
                ws.append((sem, val))
        tok = None
        incs = []
        if inc is not None:
            tok = inc.bump()
            incs = [(inc.sem, inc.step)]
        self.ops.append((ws, fn, incs))
        return tok

    def emit(self, eng):
        for ws, fn, incs in self.ops:
            for sem, val in ws:
                eng.wait_ge(sem, val)
            inst = fn(eng) if fn is not None else None
            for sem, n in incs:
                inst.then_inc(sem, n)


def build_program():
    nc = bass.Bass("TRN2", target_bir_lowering=False,
                   detect_race_conditions=False)

    dx = nc.dram_tensor("x", [T, D], F32, kind="ExternalInput").ap()
    dw = nc.dram_tensor("w_attn", [D, D], F32, kind="ExternalInput").ap()
    dmask = nc.dram_tensor("mask", [D, D], I32, kind="ExternalInput").ap()
    dwb = nc.dram_tensor("w_base", [D, D], F32, kind="ExternalInput").ap()
    dbb = nc.dram_tensor("b_base", [1, D], F32, kind="ExternalInput").ap()
    dwr = nc.dram_tensor("w_router", [8, D], F32, kind="ExternalInput").ap()
    da = nc.dram_tensor("a_all", [128, D], F32, kind="ExternalInput").ap()
    dball = nc.dram_tensor("b_all", [128, D], F32, kind="ExternalInput").ap()
    dcI = nc.dram_tensor("c_ident", [128, 128], BF16, kind="ExternalInput").ap()
    dcon = nc.dram_tensor("c_ones", [1, 128], BF16, kind="ExternalInput").ap()
    dcsx = nc.dram_tensor("c_sexp", [8, 128], BF16, kind="ExternalInput").ap()
    dout = nc.dram_tensor("y", [T, D], F32, kind="ExternalOutput").ap()

    wmd = nc.dram_tensor("wmd", [D, D], BF16, kind="Internal").ap()
    wbd = nc.dram_tensor("wbd", [D, D], BF16, kind="Internal").ap()
    boutd = nc.dram_tensor("boutd", [2, 128, 128], BF16, kind="Internal").ap()
    bttd = nc.dram_tensor("bttd", [2, 32, 128], BF16, kind="Internal").ap()

    es = contextlib.ExitStack()
    with es:
        def sb(n, sh, dt, **kw):
            return es.enter_context(nc.sbuf_tensor(n, sh, dt, **kw))
        xtok = sb("xtok", [128, NT * D], BF16)
        xT = sb("xT", [128, NKC * T], BF16, is_transpose_dest=True)
        wst = sb("wst", [128, 2 * 2048], BF16)
        wring = sb("wring", [128, RING * 512], BF16, is_transpose_dest=True)
        mst = sb("mst", [128, 2 * 1024], BF16)
        bout = sb("bout", [128, 2 * 128], BF16)
        boutT = sb("boutT", [128, 2 * 128], BF16, is_transpose_dest=True)
        bsum = sb("bsum", [128, 2 * 4], F32)
        bts = sb("bts", [128, 2 * 32], BF16)
        btt = sb("btt", [128, 2 * 32], BF16)
        expp = sb("expp", [128, 2 * 128], BF16)
        ball = sb("ball", [128, D], BF16)
        aallT = sb("aallT", [128, D], BF16)
        hp = sb("hp", [128, T], BF16)
        wrT = sb("wrT", [128, NKC * 8], BF16)
        bbias = sb("bbias", [1, D], BF16)
        ost = sb("ost", [128, 2 * 512], F32)
        cI = sb("cI", [128, 128], BF16)
        cones = sb("cones", [1, 128], BF16)
        csexp = sb("csexp", [8, 128], BF16)
        lg = sb("lg", [128, 64], F32)
        c1b = sb("c1b", [128, 64], F32)
        c2b = sb("c2b", [128, 64], F32)
        mkd = sb("mkd", [128, 64], F32)
        sg1 = sb("sg1", [128, 64], F32)
        sg2 = sb("sg2", [128, 64], F32)
        t1b = sb("t1b", [128, 64], F32)
        wq = sb("wq", [128, 64], F32)
        wqb = sb("wqb", [128, 64], BF16)
        wT8 = sb("wT8", [8, T], BF16)
        wexp = sb("wexp", [128, T], BF16)
        m1r = sb("m1r", [128, 8], F32)
        m2r = sb("m2r", [128, 8], F32)
        n1r = sb("n1r", [128, 8], F32)
        is1 = sb("is1", [128, 8], F32)
        ng1 = sb("ng1", [128, 8], F32)
        ng2 = sb("ng2", [128, 8], F32)
        sumr = sb("sumr", [128, 8], F32)
        sq2 = sb("sq2", [128, 16], F32)
        sumsq = sb("sumsq", [128, 8], F32)
        mmean = sb("mmean", [128, 8], F32)
        msq = sb("msq", [128, 8], F32)
        varr = sb("varr", [128, 8], F32)
        stdd = sb("stdd", [128, 8], F32)
        rstd = sb("rstd", [128, 8], F32)
        nmr = sb("nmr", [128, 8], F32)

        PALL = es.enter_context(nc.psum_tensor("PALL", [128, D], F32))

        qg = Ctr(es.enter_context(nc.semaphore("qg")), 16)
        qs = Ctr(es.enter_context(nc.semaphore("qs")), 16)
        qa = Ctr(es.enter_context(nc.semaphore("qa")), 16)
        cg = Ctr(es.enter_context(nc.semaphore("cg")), 1)
        cv = Ctr(es.enter_context(nc.semaphore("cv")), 1)
        ca = Ctr(es.enter_context(nc.semaphore("ca")), 1)
        cp = Ctr(es.enter_context(nc.semaphore("cp")), 1)

        G, S, A, V, P = (Stream(x) for x in "GSAVP")

        def bankview(b, lo=0, hi=512):
            return PALL[:, b * 512 + lo: b * 512 + hi]

        bankfree = {b: None for b in range(8)}
        tk = {}

        for t in range(NT):
            tk[f"xc{t}"] = G.op(lambda g, t=t: g.dma_start(
                out=xtok[:, t * D:(t + 1) * D], in_=dx[t * 128:(t + 1) * 128, :]),
                inc=qg)
        tk["bb"] = G.op(lambda g: g.dma_start(out=bbias[:], in_=dbb), inc=qg)

        tk["ball"] = G.op(lambda g: g.dma_start(out=ball[:], in_=dball), inc=qg)
        for h in range(2):
            tk[f"aL{h}"] = G.op(lambda g, h=h: g.dma_start(
                out=wring[:, h * 2048:(h + 1) * 2048],
                in_=da[:, h * 2048:(h + 1) * 2048]),
                inc=qg)

        tk["cI"] = S.op(lambda s: s.dma_start(out=cI[:], in_=dcI), inc=qs)
        tk["cones"] = S.op(lambda s: s.dma_start(out=cones[:], in_=dcon), inc=qs)
        tk["csexp"] = S.op(lambda s: s.dma_start(out=csexp[:], in_=dcsx), inc=qs)

        for g8 in range(8):
            h = g8 // 4
            last = None
            for j in range(4):
                kc = g8 * 4 + j
                off = (kc % 16) * 128 + h * 2048
                last = P.op(lambda t_, off=off, j=j: t_.matmul(
                    bankview(4, j * 128, (j + 1) * 128),
                    wring[:, off:off + 128], cI[:],
                    start=(j == 0), stop=(j == 3)),
                    waits=[tk["aL0"], tk["aL1"], tk["cI"], bankfree[4]],
                    inc=(cp if j == 3 else None))
            dst = aallT[:].rearrange("p (a b) -> p a b", b=128)[:, g8 * 4:(g8 + 1) * 4, :]
            src = bankview(4).rearrange("p (a b) -> p a b", b=128)
            bankfree[4] = V.op(lambda v, dst=dst, src=src: v.tensor_copy(dst, src),
                               waits=[last], inc=cv)
        tk["aallT"] = bankfree[4]

        for h in range(2):
            tk[f"wrL{h}"] = G.op(lambda g, h=h: g.dma_start(
                out=wring[0:8, h * 2048:h * 2048 + 2048],
                in_=dwr[:, h * 2048:(h + 1) * 2048]),
                waits=[tk["aallT"]], inc=qg)
        last = None
        for kc in range(NKC):
            h = kc // 16
            off = (kc % 16) * 128 + h * 2048
            last = P.op(lambda t_, kc=kc, off=off: t_.matmul(
                bankview(4, kc * 8, kc * 8 + 8),
                wring[0:8, off:off + 128], cI[0:8, 0:8],
                start=(kc == 0), stop=(kc == 31)),
                waits=[tk["wrL0"], tk["wrL1"], bankfree[4]],
                inc=(cp if kc == 31 else None))
        tk["wrT"] = bankfree[4] = V.op(
            lambda v: v.tensor_copy(wrT[:], bankview(4, 0, 256)),
            waits=[last], inc=cv)

        tk["btsz"] = V.op(lambda v: v.memset(bts[:], 0.0), inc=cv)



        mst_free = [None, None]
        wst_free = [None, None]
        bout_free = [None, None]
        boutT_free = [None, None]
        btt_free = [None, None]
        expp_free = [None, None]
        chunk_wmw = {}

        def plan_chunk(c):
            sl = c % 2
            tree_eng, tree_ctr = (G, cg) if c % 2 == 0 else (V, cv)
            btoks = []
            mLp = [None, None]
            for pr in range(2):
                mLp[pr] = G.op(lambda g, c=c, pr=pr: g.dma_start(
                    out=mst[:],
                    in_=dmask[c * 128:(c + 1) * 128, pr * 2048:(pr + 1) * 2048]),
                    waits=[mst_free[0], mst_free[1]], inc=qg)
                tk.setdefault(f"mfirst{c}", mLp[pr])
                for q in (2 * pr, 2 * pr + 1):
                    qsl = q % 2
                    mL = mLp[pr]
                    v3 = mst[:, qsl * 1024:(qsl + 1) * 1024].rearrange(
                        "p (a b) -> p a b", b=32)
                    if tree_eng is V:
                        def _red(e, v3=v3, c=c, q=q, sl=sl):
                            with e.bass.allow_low_precision(
                                    reason="window sums of ints, exact bf16"):
                                return e.reduce_sum(
                                    bout[:, sl * 128 + q * 32:
                                         sl * 128 + q * 32 + 32],
                                    v3, axis=AX.X)
                        tok = tree_eng.op(_red, waits=[mL, bout_free[sl]],
                                          inc=tree_ctr)
                    else:
                        tok = None
                        for step in (16, 8, 4, 2, 1):
                            w = [mL] if tok is None else [mL, tok]
                            tok = tree_eng.op(lambda e, v3=v3, step=step:
                                              e.tensor_tensor(
                                                  v3[:, :, 0:step],
                                                  v3[:, :, 0:step],
                                                  v3[:, :, step:2 * step],
                                                  ALU.add),
                                              waits=w, inc=tree_ctr)
                        tok = tree_eng.op(lambda e, v3=v3, c=c, q=q, sl=sl:
                                          e.tensor_copy(
                                              bout[:, sl * 128 + q * 32:
                                                   sl * 128 + q * 32 + 32],
                                              v3[:, :, 0]),
                                          waits=[tok, bout_free[sl]],
                                          inc=tree_ctr)
                    mst_free[qsl] = tok
                    btoks.append(tok)
            if tree_eng is V:
                btoks.append(V.op(lambda v: v.memset(bsum[:, sl * 4:sl * 4 + 1],
                                                     0.0),
                                  waits=[btoks[-1]], inc=cv))
            t_bw = A.op(lambda a, c=c: a.dma_start(
                out=boutd[sl], in_=bout[:, sl * 128:sl * 128 + 128]),
                waits=btoks, inc=qa)
            t_btl = S.op(lambda s, c=c: s.dma_start(
                out=boutT[:, sl * 128:sl * 128 + 128], in_=boutd[sl],
                transpose=True), waits=[t_bw, boutT_free[sl]], inc=qs)
            bv = boutT[:, sl * 128:sl * 128 + 128].rearrange("p (a b) -> p a b", b=32)
            t_rs = V.op(lambda v, bv=bv, c=c: v.reduce_sum(
                bsum[:, sl * 4:sl * 4 + 4], bv, axis=AX.X),
                waits=[t_btl], inc=cv)
            boutT_free[sl] = t_rs
            t_th = V.op(lambda v, c=c: v.tensor_scalar(
                bts[:, sl * 32:sl * 32 + 4],
                bsum[:, sl * 4:sl * 4 + 4], 0.0, None, ALU.is_gt),
                waits=[t_rs, tk["btsz"]], inc=cv)
            t_tt = V.op(lambda v, c=c: v.transpose(
                btt[:, sl * 32:sl * 32 + 32], bts[:, sl * 32:sl * 32 + 32]),
                waits=[t_th, btt_free[sl]], inc=cv)
            t_tw = A.op(lambda a, c=c: a.dma_start(
                out=bttd[sl].rearrange("nb (qq j) -> qq nb j", j=32),
                in_=btt[:, sl * 32:sl * 32 + 32]),
                waits=[t_tt], inc=qa)
            btt_free[sl] = t_tw
            tk[f"ttw{c}"] = t_tw
            bout_free[sl] = t_bw

        def plan_chunk_w(c):
            sl = c % 2
            src = bttd[sl][0:4, :].unsqueeze(1).broadcast_to([4, 32, 128])
            t_ex = A.op(lambda a, c=c, src=src: a.dma_start(
                out=expp[:, sl * 128:sl * 128 + 128], in_=src),
                waits=[tk[f"ttw{c}"], expp_free[sl]], inc=qa)
            mm_last = None
            t_wl = G.op(lambda g, c=c: g.dma_start(
                out=wst[:], in_=dw[c * 128:(c + 1) * 128, :]),
                waits=[wst_free[0], wst_free[1]], inc=qg)
            for h in range(2):
                wv = wst[:, h * 2048:(h + 1) * 2048].rearrange("p (a b) -> p a b", b=32)
                ev = expp[:, sl * 128 + h * 64:sl * 128 + h * 64 + 64] \
                    .unsqueeze(2).broadcast_to([128, 64, 32])
                mm_last = V.op(lambda v, wv=wv, ev=ev: v.tensor_tensor(
                    wv, wv, ev, ALU.mult), waits=[t_wl, t_ex], inc=cv)
                t_ww = A.op(lambda a, c=c, h=h: a.dma_start(
                    out=wmd[c * 128:(c + 1) * 128, h * 2048:(h + 1) * 2048],
                    in_=wst[:, h * 2048:(h + 1) * 2048]),
                    waits=[mm_last], inc=qa)
                wst_free[h] = t_ww
                chunk_wmw[(c, h)] = t_ww
            expp_free[sl] = mm_last

        for c in range(8):
            plan_chunk(c)
            plan_chunk_w(c)

        for g8 in range(8):
            for tau in range(8):
                bk = 4 + (tau % 2)
                last = None
                for j in range(4):
                    kc = g8 * 4 + j
                    guard = tk[f"xc{tau + 1}"] if tau < 7 else tk["mfirst0"]
                    last = P.op(lambda t_, tau=tau, kc=kc, j=j, bk=bk: t_.matmul(
                        bankview(bk, j * 128, (j + 1) * 128),
                        xtok[:, tau * D + kc * 128:tau * D + kc * 128 + 128],
                        cI[:], start=(j == 0), stop=(j == 3)),
                        waits=[tk[f"xc{tau}"], guard, tk["cI"], bankfree[bk]],
                        inc=(cp if j == 3 else None))
                dst = xT[:].rearrange("p (a b) -> p a b", b=T)[
                    :, g8 * 4:(g8 + 1) * 4, tau * 128:(tau + 1) * 128]
                psrc = bankview(bk).rearrange("p (a b) -> p a b", b=128)
                tok = V.op(lambda v, dst=dst, psrc=psrc: v.tensor_copy(dst, psrc),
                           waits=[last], inc=cv)
                bankfree[bk] = tok
            tk[f"xTg{g8}"] = tok


        ringfree = [None] * RING
        ring_tok = {}

        def ring_load(dram, gidx, s, kc, extra):
            slot = gidx % RING
            w = list(extra) + [ringfree[slot]]
            if gidx < RING:
                w.append(tk["wrT"])
            ring_tok[gidx] = S.op(
                lambda sy, dram=dram, s=s, kc=kc, slot=slot: sy.dma_start(
                    out=wring[:, slot * 512:(slot + 1) * 512],
                    in_=dram[s * 512:(s + 1) * 512, kc * 128:(kc + 1) * 128],
                    transpose=True),
                waits=w, inc=qs)

        aev = {}
        next_chunk = 8
        wbd_piece = [0]
        for s in range(NSW):
            for i in range(NKC + PF):
                if i < NKC:
                    if s == 0:
                        ws = [chunk_wmw[(c, h)] for c in range(4) for h in range(2)]
                        ws = [chunk_wmw[(3, 1)]]
                    else:
                        ws = [chunk_wmw[(4 * s + 3, 1)]]
                    ring_load(wmd, s * NKC + i, s, i, ws)
                j = i - PF
                if 0 <= j < NKC:
                    gidx = s * NKC + j
                    slot = gidx % RING
                    tok = None
                    for tau in range(8):
                        w = [ring_tok[gidx]]
                        if s == 0:
                            w.append(tk[f"xTg{j // 4}"])
                        if j == 0:
                            w.append(bankfree[tau])
                        tok = P.op(lambda t_, tau=tau, j=j, slot=slot: t_.matmul(
                            bankview(tau),
                            xT[:, j * T + tau * 128:j * T + tau * 128 + 128],
                            wring[:, slot * 512:(slot + 1) * 512],
                            start=(j == 0), stop=(j == NKC - 1)),
                            waits=w, inc=(cp if tau == 7 else None))
                    ringfree[slot] = tok
                    if j == NKC - 1:
                        tk[f"amm{s}"] = tok
                if i in (2, 10, 18, 26) and next_chunk < NCH:
                    plan_chunk(next_chunk)
                if i in (6, 14, 22, 30) and next_chunk < NCH:
                    plan_chunk_w(next_chunk)
                    next_chunk += 1
                if s >= 1 and i in (3, 9, 15, 21, 27) and wbd_piece[0] < 32:
                    pp = wbd_piece[0]
                    wbd_piece[0] += 1
                    tok = G.op(lambda g, pp=pp: g.dma_start(
                        out=wbd[pp * 128:(pp + 1) * 128, :],
                        in_=dwb[pp * 128:(pp + 1) * 128, :]),
                        inc=qg)
                    if pp % 4 == 3:
                        tk[f"wbd{pp // 4}"] = tok

            for tau in range(8):
                dst = xtok[:, tau * D + s * 512:tau * D + s * 512 + 512]
                tok = V.op(lambda v, tau=tau, dst=dst: v.scalar_tensor_tensor(
                    dst, bankview(tau), 1.0, dst, ALU.mult, ALU.add),
                    waits=[tk[f"amm{s}"]], inc=cv)
                aev[(s, tau)] = tok
                bankfree[tau] = tok

        ln1 = {}
        sqt = {}
        for tau in range(8):
            w_data = [aev[(NSW - 1, tau)]]
            for h in range(2):
                A.op(lambda a, tau=tau, h=h: a.activation(
                    PALL[:, 0:2048],
                    xtok[:, tau * D + h * 2048:tau * D + h * 2048 + 2048],
                    AF.Square, accum_out=sq2[:, 2 * tau + h:2 * tau + h + 1]),
                    waits=w_data + [bankfree[b] for b in range(4)], inc=ca)
            sqt[tau] = (ca.sem, ca.v)
        srt = {}
        for tau in range(8):
            srt[tau] = V.op(lambda v, tau=tau: v.reduce_sum(
                sumr[:, tau:tau + 1], xtok[:, tau * D:(tau + 1) * D], axis=AX.X),
                waits=[aev[(NSW - 1, tau)]], inc=cv)
        for tau in range(8):
            V.op(lambda v, tau=tau: v.tensor_tensor(
                sumsq[:, tau:tau + 1], sq2[:, 2 * tau:2 * tau + 1],
                sq2[:, 2 * tau + 1:2 * tau + 2], ALU.add),
                waits=[sqt[tau]], inc=cv)
        for tau in range(8):
            V.op(lambda v, tau=tau: v.tensor_scalar(
                mmean[:, tau:tau + 1], sumr[:, tau:tau + 1], 1.0 / D, None,
                ALU.mult), waits=[srt[tau]], inc=cv)
        for tau in range(8):
            V.op(lambda v, tau=tau: v.tensor_scalar(
                msq[:, tau:tau + 1], mmean[:, tau:tau + 1], mmean[:, tau:tau + 1],
                EPS, ALU.mult, ALU.subtract), inc=cv)
        vvt = {}
        for tau in range(8):
            vvt[tau] = V.op(lambda v, tau=tau: v.tensor_scalar(
                varr[:, tau:tau + 1], sumsq[:, tau:tau + 1], 1.0 / D,
                msq[:, tau:tau + 1], ALU.mult, ALU.subtract), inc=cv)
        sdt = {}
        for tau in range(8):
            sdt[tau] = A.op(lambda a, tau=tau: a.activation(
                stdd[:, tau:tau + 1], varr[:, tau:tau + 1], AF.Sqrt),
                waits=[vvt[tau]], inc=ca)
        for tau in range(8):
            V.op(lambda v, tau=tau: v.reciprocal(
                rstd[:, tau:tau + 1], stdd[:, tau:tau + 1]),
                waits=[sdt[tau]], inc=cv)
        for tau in range(8):
            V.op(lambda v, tau=tau: v.tensor_scalar(
                nmr[:, tau:tau + 1], mmean[:, tau:tau + 1],
                rstd[:, tau:tau + 1], -1.0, ALU.mult, ALU.mult), inc=cv)
        for tau in range(8):
            ln1[tau] = V.op(lambda v, tau=tau: v.tensor_scalar(
                xtok[:, tau * D:(tau + 1) * D], xtok[:, tau * D:(tau + 1) * D],
                rstd[:, tau:tau + 1], nmr[:, tau:tau + 1], ALU.mult, ALU.add),
                inc=cv)
        for b in range(4):
            bankfree[b] = (ca.sem, ca.v)

        y1T = {}
        for tau in range(8):
            for g8 in range(8):
                bk = 4 + (g8 % 2)
                last = None
                for j in range(4):
                    kc = g8 * 4 + j
                    last = P.op(lambda t_, tau=tau, kc=kc, j=j, bk=bk: t_.matmul(
                        bankview(bk, j * 128, (j + 1) * 128),
                        xtok[:, tau * D + kc * 128:tau * D + kc * 128 + 128],
                        cI[:], start=(j == 0), stop=(j == 3)),
                        waits=[ln1[tau], bankfree[bk], tk[f"amm{NSW - 1}"]],
                        inc=(cp if j == 3 else None))
                dst = xT[:].rearrange("p (a b) -> p a b", b=T)[
                    :, g8 * 4:(g8 + 1) * 4, tau * 128:(tau + 1) * 128]
                src = bankview(bk).rearrange("p (a b) -> p a b", b=128)
                tok = V.op(lambda v, dst=dst, src=src: v.tensor_copy(dst, src),
                           waits=[last], inc=cv)
                y1T[(tau, g8)] = tok
                bankfree[bk] = tok

        tok = None
        for tau in range(8):
            for kc in range(NKC):
                tok = P.op(lambda t_, tau=tau, kc=kc: t_.matmul(
                    PALL[:, 3072 + tau * 8:3072 + tau * 8 + 8],
                    xT[:, kc * T + tau * 128:kc * T + tau * 128 + 128],
                    wrT[:, kc * 8:(kc + 1) * 8],
                    start=(kc == 0), stop=(kc == NKC - 1)),
                    waits=[y1T[(tau, kc // 4)], tk["wrT"],
                           bankfree[6] if (tau == 0 and kc == 0) else None],
                    inc=(cp if (tau == 7 and kc == NKC - 1) else None))
        t_lg = V.op(lambda v: v.tensor_copy(lg[:], PALL[:, 3072:3136]),
                    waits=[tok], inc=cv)
        bankfree[6] = t_lg
        wtok = {}
        for tau in range(8):
            c8 = slice(tau * 8, tau * 8 + 8)
            tc = slice(tau, tau + 1)
            t1 = V.op(lambda v, c8=c8, tc=tc: v.reduce_max(
                m1r[:, tc], lg[:, c8], axis=AX.X), waits=[t_lg], inc=cv)
            t2 = V.op(lambda v, c8=c8, tc=tc: v.tensor_scalar(
                c1b[:, c8], lg[:, c8], m1r[:, tc], None, ALU.is_equal),
                waits=[t1], inc=cv)
            t3 = V.op(lambda v, c8=c8: v.scalar_tensor_tensor(
                mkd[:, c8], c1b[:, c8], NEG_BIG, lg[:, c8], ALU.mult, ALU.add),
                waits=[t2], inc=cv)
            t4 = V.op(lambda v, c8=c8, tc=tc: v.reduce_max(
                m2r[:, tc], mkd[:, c8], axis=AX.X), waits=[t3], inc=cv)
            t5 = V.op(lambda v, c8=c8, tc=tc: v.reduce_sum(
                n1r[:, tc], c1b[:, c8], axis=AX.X), waits=[t2], inc=cv)
            t6 = V.op(lambda v, tc=tc: v.tensor_scalar(
                is1[:, tc], n1r[:, tc], 1.0, None, ALU.is_equal),
                waits=[t5], inc=cv)
            t7 = V.op(lambda v, tc=tc: v.tensor_scalar(
                ng2[:, tc], m2r[:, tc], -1.0, None, ALU.mult), waits=[t4], inc=cv)
            t8 = V.op(lambda v, tc=tc: v.tensor_scalar(
                ng1[:, tc], m1r[:, tc], -1.0, None, ALU.mult), waits=[t1], inc=cv)
            t9 = A.op(lambda a, c8=c8, tc=tc: a.activation(
                sg1[:, c8], lg[:, c8], AF.Sigmoid, bias=ng2[:, tc]),
                waits=[t7], inc=ca)
            t10 = A.op(lambda a, c8=c8, tc=tc: a.activation(
                sg2[:, c8], lg[:, c8], AF.Sigmoid, bias=ng1[:, tc]),
                waits=[t8], inc=ca)
            t11 = V.op(lambda v, c8=c8, tc=tc: v.tensor_scalar(
                c2b[:, c8], mkd[:, c8], m2r[:, tc], None, ALU.is_equal),
                waits=[t4], inc=cv)
            t12 = V.op(lambda v, c8=c8, tc=tc: v.tensor_scalar(
                t1b[:, c8], sg1[:, c8], -0.5, is1[:, tc], ALU.add, ALU.mult),
                waits=[t9, t6], inc=cv)
            t13 = V.op(lambda v, c8=c8: v.tensor_scalar(
                t1b[:, c8], t1b[:, c8], 0.5, None, ALU.add), waits=[t12], inc=cv)
            t14 = V.op(lambda v, c8=c8: v.tensor_tensor(
                t1b[:, c8], c1b[:, c8], t1b[:, c8], ALU.mult), waits=[t13], inc=cv)
            t15 = V.op(lambda v, c8=c8, tc=tc: v.scalar_tensor_tensor(
                c2b[:, c8], c2b[:, c8], is1[:, tc], sg2[:, c8], ALU.mult, ALU.mult),
                waits=[t11, t10], inc=cv)
            t16 = V.op(lambda v, c8=c8: v.tensor_tensor(
                wq[:, c8], t1b[:, c8], c2b[:, c8], ALU.add),
                waits=[t14, t15], inc=cv)
            wtok[tau] = V.op(lambda v, c8=c8: v.tensor_copy(wqb[:, c8], wq[:, c8]),
                             waits=[t16], inc=cv)

        tok = None
        for tau in range(8):
            bk_off = 3072 + tau * 128
            tok = P.op(lambda t_, tau=tau, bk_off=bk_off: t_.matmul(
                PALL[0:8, bk_off:bk_off + 128],
                wqb[:, tau * 8:(tau + 1) * 8], cI[:],
                start=(tau in (0, 4)), stop=(tau in (3, 7))),
                waits=[wtok[tau], bankfree[6], bankfree[7]],
                inc=(cp if tau == 7 else None))
        t_wT = V.op(lambda v: v.tensor_copy(wT8[:], PALL[0:8, 3072:4096]),
                    waits=[tok], inc=cv)
        bankfree[6] = bankfree[7] = t_wT
        xtoks = []
        for j in range(2):
            tok = P.op(lambda t_, j=j: t_.matmul(
                PALL[:, 3072 + j * 512:3072 + (j + 1) * 512],
                csexp[:], wT8[:, j * 512:(j + 1) * 512],
                start=True, stop=True),
                waits=[t_wT, tk["csexp"], bankfree[6 + j]], inc=cp)
            xtoks.append(tok)
        t_we = V.op(lambda v: v.tensor_copy(wexp[:], PALL[:, 3072:4096]),
                    waits=xtoks, inc=cv)
        bankfree[6] = bankfree[7] = t_we
        tok = None
        for kc in range(NKC):
            for j in range(2):
                tok = P.op(lambda t_, kc=kc, j=j: t_.matmul(
                    PALL[:, 3072 + j * 512:3072 + (j + 1) * 512],
                    aallT[:, kc * 128:(kc + 1) * 128],
                    xT[:, kc * T + j * 512:kc * T + j * 512 + 512],
                    start=(kc == 0), stop=(kc == NKC - 1)),
                    waits=[tk["aallT"], y1T[(7, 7)], t_we],
                    inc=(cp if (kc == NKC - 1 and j == 1) else None))
        hp_last = None
        for j in range(2):
            hp_last = V.op(lambda v, j=j: v.tensor_tensor(
                hp[:, j * 512:(j + 1) * 512],
                PALL[:, 3072 + j * 512:3072 + (j + 1) * 512],
                wexp[:, j * 512:(j + 1) * 512], ALU.mult),
                waits=[tok], inc=cv)
        bankfree[6] = bankfree[7] = hp_last
        tk["hp"] = hp_last

        fev = {}
        for s in range(NSW):
            for i in range(NKC):
                gidx = 256 + s * NKC + i
                ring_load(wbd, gidx, s, i, [tk[f"wbd{s}"]])
                if i == 0:
                    for tau in range(8):
                        P.op(lambda t_, tau=tau, s=s: t_.matmul(
                            bankview(tau), cones[:],
                            bbias[0:1, s * 512:(s + 1) * 512],
                            start=True, stop=False),
                            waits=[tk["bb"], tk["cones"], bankfree[tau]], inc=None)
                j = i - PF
                if 0 <= j < NKC:
                    gj = 256 + s * NKC + j
                    slot = gj % RING
                    tok = None
                    for tau in range(8):
                        tok = P.op(lambda t_, tau=tau, j=j, slot=slot: t_.matmul(
                            bankview(tau),
                            xT[:, j * T + tau * 128:j * T + tau * 128 + 128],
                            wring[:, slot * 512:(slot + 1) * 512],
                            start=False, stop=False),
                            waits=[ring_tok[gj]], inc=(cp if tau == 7 else None))
                    ringfree[slot] = tok
            for j in range(NKC - PF, NKC):
                gj = 256 + s * NKC + j
                slot = gj % RING
                tok = None
                for tau in range(8):
                    tok = P.op(lambda t_, tau=tau, j=j, slot=slot: t_.matmul(
                        bankview(tau),
                        xT[:, j * T + tau * 128:j * T + tau * 128 + 128],
                        wring[:, slot * 512:(slot + 1) * 512],
                        start=False, stop=False),
                        waits=[ring_tok[gj]], inc=(cp if tau == 7 else None))
                ringfree[slot] = tok
            for tau in range(8):
                tok = P.op(lambda t_, tau=tau, s=s: t_.matmul(
                    bankview(tau), hp[:, tau * 128:(tau + 1) * 128],
                    ball[:, s * 512:(s + 1) * 512],
                    start=False, stop=True),
                    waits=[tk["hp"], tk["ball"]], inc=(cp if tau == 7 else None))
            tk[f"fmm{s}"] = tok
            for tau in range(8):
                dst = xtok[:, tau * D + s * 512:tau * D + s * 512 + 512]
                tok = V.op(lambda v, tau=tau, dst=dst: v.scalar_tensor_tensor(
                    dst, bankview(tau), 1.0, dst, ALU.mult, ALU.add),
                    waits=[tk[f"fmm{s}"]], inc=cv)
                fev[(s, tau)] = tok
                bankfree[tau] = tok

        ost_free = [None, None]
        sqt = {}
        for tau in range(8):
            w_data = [fev[(NSW - 1, tau)], fev[(NSW - 1, 7)]]
            for h in range(2):
                A.op(lambda a, tau=tau, h=h: a.activation(
                    PALL[:, 0:2048],
                    xtok[:, tau * D + h * 2048:tau * D + h * 2048 + 2048],
                    AF.Square, accum_out=sq2[:, 2 * tau + h:2 * tau + h + 1]),
                    waits=w_data + [bankfree[b] for b in range(4)], inc=ca)
            sqt[tau] = (ca.sem, ca.v)
        srt = {}
        for tau in range(8):
            srt[tau] = V.op(lambda v, tau=tau: v.reduce_sum(
                sumr[:, tau:tau + 1], xtok[:, tau * D:(tau + 1) * D], axis=AX.X),
                waits=[fev[(NSW - 1, tau)]], inc=cv)
        for tau in range(8):
            V.op(lambda v, tau=tau: v.tensor_tensor(
                sumsq[:, tau:tau + 1], sq2[:, 2 * tau:2 * tau + 1],
                sq2[:, 2 * tau + 1:2 * tau + 2], ALU.add),
                waits=[sqt[tau]], inc=cv)
        for tau in range(8):
            V.op(lambda v, tau=tau: v.tensor_scalar(
                mmean[:, tau:tau + 1], sumr[:, tau:tau + 1], 1.0 / D, None,
                ALU.mult), waits=[srt[tau]], inc=cv)
        for tau in range(8):
            V.op(lambda v, tau=tau: v.tensor_scalar(
                msq[:, tau:tau + 1], mmean[:, tau:tau + 1], mmean[:, tau:tau + 1],
                EPS, ALU.mult, ALU.subtract), inc=cv)
        vvt = {}
        for tau in range(8):
            vvt[tau] = V.op(lambda v, tau=tau: v.tensor_scalar(
                varr[:, tau:tau + 1], sumsq[:, tau:tau + 1], 1.0 / D,
                msq[:, tau:tau + 1], ALU.mult, ALU.subtract), inc=cv)
        sdt = {}
        for tau in range(8):
            sdt[tau] = A.op(lambda a, tau=tau: a.activation(
                stdd[:, tau:tau + 1], varr[:, tau:tau + 1], AF.Sqrt),
                waits=[vvt[tau]], inc=ca)
        for tau in range(8):
            V.op(lambda v, tau=tau: v.reciprocal(
                rstd[:, tau:tau + 1], stdd[:, tau:tau + 1]),
                waits=[sdt[tau]], inc=cv)
        nmt = {}
        for tau in range(8):
            nmt[tau] = V.op(lambda v, tau=tau: v.tensor_scalar(
                nmr[:, tau:tau + 1], mmean[:, tau:tau + 1],
                rstd[:, tau:tau + 1], -1.0, ALU.mult, ALU.mult), inc=cv)
        for tau in range(8):
            for q in range(8):
                slq = q % 2
                t_n = V.op(lambda v, tau=tau, q=q, slq=slq: v.tensor_scalar(
                    ost[:, slq * 512:(slq + 1) * 512],
                    xtok[:, tau * D + q * 512:tau * D + q * 512 + 512],
                    rstd[:, tau:tau + 1], nmr[:, tau:tau + 1],
                    ALU.mult, ALU.add),
                    waits=[nmt[tau], ost_free[slq]], inc=cv)
                t_st = A.op(lambda a, tau=tau, q=q, slq=slq: a.dma_start(
                    out=dout[tau * 128:(tau + 1) * 128, q * 512:q * 512 + 512],
                    in_=ost[:, slq * 512:(slq + 1) * 512]),
                    waits=[t_n], inc=qa)
                ost_free[slq] = t_st
        A.op(None, waits=[ost_free[0], ost_free[1]])

        blk = es.enter_context(nc.Block())
        blk.gpsimd(lambda g: G.emit(g))
        blk.sync(lambda s: S.emit(s))
        blk.scalar(lambda a: A.emit(a))
        blk.vector(lambda v: V.emit(v))
        blk.tensor(lambda t_: P.emit(t_))

    return nc


_NC_CACHE = []


def kernel(L0_attn_data, L0_attn_weight, L0_attn_mask, L0_ffn_W_base,
           L0_ffn_b_base, L0_ffn_W_router, L0_ffn_lora_A, L0_ffn_lora_B):
    x = np.ascontiguousarray(np.asarray(L0_attn_data, dtype=np.float32)
                             .reshape(8192, 4096))
    w = np.ascontiguousarray(np.asarray(L0_attn_weight, dtype=np.float32))
    mask = np.ascontiguousarray(np.asarray(L0_attn_mask, dtype=np.int32))
    wb = np.ascontiguousarray(np.asarray(L0_ffn_W_base, dtype=np.float32))
    bb = np.ascontiguousarray(np.asarray(L0_ffn_b_base, dtype=np.float32)
                              .reshape(1, 4096))
    wr = np.ascontiguousarray(np.asarray(L0_ffn_W_router, dtype=np.float32))
    a_all = np.ascontiguousarray(np.asarray(L0_ffn_lora_A, dtype=np.float32)
                                 .reshape(128, 4096))
    b_all = np.ascontiguousarray(np.asarray(L0_ffn_lora_B, dtype=np.float32)
                                 .transpose(0, 2, 1).reshape(128, 4096))
    c_ident = np.eye(128, dtype=np.float32).astype(BF)
    c_ones = np.ones((1, 128), dtype=np.float32).astype(BF)
    c_sexp = np.zeros((8, 128), dtype=np.float32)
    for r in range(128):
        c_sexp[r // 16, r] = 2.0
    c_sexp = c_sexp.astype(BF)

    if not _NC_CACHE:
        _NC_CACHE.append(build_program())
    nc = _NC_CACHE[0]

    in_maps = []
    for c in range(NCORES):
        in_maps.append({
            "x": np.ascontiguousarray(x[c * T:(c + 1) * T]),
            "w_attn": w, "mask": mask, "w_base": wb, "b_base": bb,
            "w_router": wr, "a_all": a_all, "b_all": b_all,
            "c_ident": c_ident, "c_ones": c_ones, "c_sexp": c_sexp,
        })
    res = run_bass_kernel_spmd(nc, in_maps, core_ids=list(range(NCORES)))
    out = np.concatenate([np.asarray(res.results[c]["y"]) for c in range(NCORES)],
                         axis=0)
    return out.reshape(1, 8192, 4096)
